# version 08 (final)
# baseline (speedup 1.0000x reference)
import contextlib
from collections import defaultdict

import numpy as np
import ml_dtypes

import concourse.bass as bass
import concourse.mybir as mybir
from concourse.bass_utils import run_bass_kernel_spmd

F32 = mybir.dt.float32
BF16 = mybir.dt.bfloat16
I32 = mybir.dt.int32
AF = mybir.ActivationFunctionType
ALU = mybir.AluOpType
AX = mybir.AxisListType

ENGINES = ("sync", "scalar", "vector", "gpsimd", "tensor")


class Cfg:
    def __init__(self, L=8192, TOWN=2048, TC=2048, PSL=512, WIN=512):
        self.L = L
        self.TOWN = TOWN
        self.WIN = WIN
        self.TW = WIN + TOWN
        self.D = 1024
        self.DB = 256
        self.N = 128
        self.R = 64
        self.HD = 64
        self.NH = 16
        self.NKV = 4
        self.F2 = 8192
        self.F = 4096
        self.FI = 2816
        self.SD = 4
        self.SN = 32
        self.OB = self.D // self.SD
        self.NB = self.N // self.SN
        self.TC = TC
        self.PSL = PSL
        self.QC = min(1024, TOWN)
        self.S8L = min(512, TOWN)
        self.DCH = self.D // 128
        self.NSL = L // PSL
        self.NCH = L // 128
        self.EPS_RMS = 1e-6
        self.EPS_LN = 1e-5
        self.SCT = 8
        self.TPE = 32
        assert L % TC == 0 and L % PSL == 0 and TOWN % self.QC == 0
        assert TOWN % 128 == 0 and TOWN % self.S8L == 0
        assert self.TW % self.PSL == 0 and WIN % 128 == 0
        assert WIN >= 128 and TOWN % self.TPE == 0
        assert self.TPE % self.SCT == 0 and 128 % self.SCT == 0


class Sched:

    def __init__(self, nc, es):
        self.nc = nc
        self.es = es
        self.sem = {}
        self.cnt = defaultdict(int)
        self.hw = defaultdict(int)
        self.ops = {e: [] for e in ENGINES}
        self.fences = []

    def ch(self, name):
        if name not in self.sem:
            self.sem[name] = self.es.enter_context(self.nc.semaphore(name))
        return self.sem[name]

    def fence(self, evs):
        for e in evs:
            if e is not None:
                self.fences.append(e)

    def op(self, eng, fn, waits=(), incs=()):
        agg = {}
        for w in list(self.fences) + list(waits):
            if w is None:
                continue
            c, v = w
            if v > agg.get(c, -1):
                agg[c] = v
        real = []
        for c, v in agg.items():
            if self.hw[(eng, c)] >= v:
                continue
            self.hw[(eng, c)] = v
            real.append((self.ch(c), v))
        ev = None
        incs = list(incs)
        vals = []
        for c, n in incs:
            self.cnt[c] += n
            vals.append(self.cnt[c])
            ev = (c, self.cnt[c])

        def emit(e, _fn=fn, _w=tuple(real), _incs=tuple(incs)):
            for s, v in _w:
                e.wait_ge(s, v)
            inst = _fn(e)
            for c, n in _incs:
                inst.then_inc(self.ch(c), n)
        self.ops[eng].append(emit)
        return ev

    def sgnmask_col(self):
        if not hasattr(self, "_sgn_t"):
            self._sgn_t = self.es.enter_context(
                self.nc.sbuf_tensor("c_sgnmask", [128, 2], I32))
            self.op("vector", lambda e: e.memset(
                self._sgn_t[:, 0:1], -0x80000000), incs=[("dve", 1)])
            self._sgn_ev = self.op("vector", lambda e: e.memset(
                self._sgn_t[:, 1:2], 0x7FFFFFFF), incs=[("dve", 1)])
        return self._sgn_t[:, 0:1]

    def posmask_col(self):
        self.sgnmask_col()
        return self._sgn_t[:, 1:2]

    def dma(self, queue, out, in_, waits=(), ch=None, **kw):
        return self.op(queue, lambda e: e.dma_start(out=out, in_=in_, **kw),
                       waits=waits, incs=[(ch or ("q_" + queue), 16)])

    def dmat(self, queue, out, in_, waits=(), ch=None):
        return self.op(queue, lambda e: e.dma_start(out=out, in_=in_, transpose=True),
                       waits=waits, incs=[(ch or ("q_" + queue), 16)])



def _bf(x):
    return np.ascontiguousarray(np.asarray(x)).astype(ml_dtypes.bfloat16)


def _f32(x):
    return np.ascontiguousarray(np.asarray(x), dtype=np.float32)


def _perm(D, db0, DB):
    blocks = np.arange(D // 16)
    own = blocks[(blocks * 16 >= db0) & (blocks * 16 < db0 + DB)]
    rest = blocks[~((blocks * 16 >= db0) & (blocks * 16 < db0 + DB))]
    order = np.concatenate([own, rest])
    idx = (order[:, None] * 16 + np.arange(16)[None, :]).reshape(-1)
    return idx


def _rot_mat(hd=64, nblk=2):
    P = np.zeros((hd * nblk, hd * nblk), np.float32)
    for b in range(nblk):
        o = b * hd
        for j in range(hd // 2):
            P[o + j + hd // 2, o + j] = -1.0
        for j in range(hd // 2, hd):
            P[o + j - hd // 2, o + j] = 1.0
    return P


def _red_mat(p, groups):
    R = np.zeros((p, groups), np.float32)
    for i in range(p):
        R[i, i // (p // groups)] = 1.0
    return R


def host_inputs(cfg, inputs):
    L, TOWN, D, N, HD, WIN = cfg.L, cfg.TOWN, cfg.D, cfg.N, cfg.HD, cfg.WIN
    x = _f32(inputs["L0_attn_x"])
    inv = 1.0 / (10000.0 ** (np.arange(0, HD, 2, dtype=np.float64) / HD))
    freqs = np.arange(L, dtype=np.float64)[:, None] * inv[None, :]
    emb = np.concatenate([freqs, freqs], -1)
    cosT = np.asarray(np.cos(emb).T, np.float32)
    sinT = np.asarray(np.sin(emb).T, np.float32)
    cosK = _f32(np.concatenate([cosT, cosT], 0))
    sinK = _f32(np.concatenate([sinT, sinT], 0))
    A_log = _f32(inputs["L0_attn_A_log"])

    g = lambda n: _f32(inputs[n])
    maps = []
    for c in range(8):
        b, qi = divmod(c, 4)
        own = slice(qi * TOWN, (qi + 1) * TOWN)
        xw = np.zeros((cfg.TW, D), np.float32)
        w0 = max(0, qi * TOWN - WIN)
        xw[cfg.TW - ((qi + 1) * TOWN - w0):] = x[b, w0:(qi + 1) * TOWN]
        Ap = np.zeros((128, cfg.OB * cfg.NB), np.float32)
        di = np.arange(128) // cfg.SN
        nj = np.arange(128) % cfg.SN
        for o in range(cfg.OB):
            for nu in range(cfg.NB):
                Ap[:, nu * cfg.OB + o] = A_log[di * 256 + o, nu * cfg.SN + nj]
        r8 = np.zeros((128, 1024), np.float32)
        for m8 in range(16):
            for p8 in range(128):
                r8[p8, m8 * 64 + 4 * m8 + p8 // 32] = 1.0
        m = dict(
            x_tok=_f32(x[b]),
            x_win=xw,
            xT_own=_f32(x[b, own].T),
            Wq=g("L0_attn_Wq"), Wo=g("L0_attn_Wo"), Wout=g("L0_attn_Wout"),
            W1=g("L0_attn_W1"), W2=g("L0_attn_W2"),
            Wk=g("L0_attn_Wk"), Wv=g("L0_attn_Wv"), Wx=g("L0_attn_Wx"),
            Wsel=g("L0_attn_Wsel"), Wimp=g("L0_attn_Wimp"),
            Wg=g("L0_attn_Wg"), Wdt=g("L0_attn_Wdt"),
            wg4=g("L0_ffn_w_gate"), wu4=g("L0_ffn_w_up"), wd4=g("L0_ffn_w_down"),
            w_attn=g("L0_attn_norm_attn_w").reshape(D, 1),
            w_ssm=g("L0_attn_ssm_norm_w").reshape(D, 1),
            w_ffn=g("L0_attn_norm_ffn_w").reshape(D, 1),
            qn_w=g("L0_attn_qn_w").reshape(HD, 1),
            kn_w=g("L0_attn_kn_w").reshape(HD, 1),
            qn_rot=_f32(np.roll(g("L0_attn_qn_w"), -32)).reshape(HD, 1),
            kn_rot=_f32(np.roll(g("L0_attn_kn_w"), -32)).reshape(HD, 1),
            bdt=g("L0_attn_bdt").reshape(D, 1),
            bsel=g("L0_attn_bsel").reshape(D, 1),
            selg=g("L0_attn_sel_gate").reshape(D, 1),
            bimp=g("L0_attn_bimp").reshape(N, 1),
            bg=g("L0_attn_bg").reshape(D, 1),
            Dp=g("L0_attn_Dp").reshape(D, 1),
            A_pairs_log=Ap,
            cosK=cosK, sinK=sinK,
            cosQ=_f32(cosK[:, own]), sinQ=_f32(sinK[:, own]),
            ident=_bf(np.eye(128, dtype=np.float32)),
            identf=np.eye(128, dtype=np.float32),
            rot=_bf(_rot_mat()),
            red2=_bf(_red_mat(128, 2)),
            bcast2=_bf(_red_mat(128, 2).T),
            bcast2f=_f32(_red_mat(128, 2).T),
            r8st=_bf(r8),
            onesrow=_bf(np.ones((1, 128), np.float32)),
            onescol=_bf(np.ones((128, 1), np.float32)),
        )
        maps.append(m)
    return maps


def kernel(**inputs):
    cfg = Cfg()
    maps = host_inputs(cfg, inputs)
    nc = build(cfg)
    res = run_bass_kernel_spmd(nc, maps, core_ids=list(range(8)))
    out = np.empty((2, cfg.L, cfg.D), np.float32)
    for c in range(8):
        b, qi = divmod(c, 4)
        o = np.asarray(res.results[c]["outT"])
        out[b, qi * cfg.TOWN:(qi + 1) * cfg.TOWN, :] = o.T
    return out



def build(cfg, upto=9, dbg=False):
    nc = bass.Bass("TRN2", target_bir_lowering=False, debug=False)
    es = contextlib.ExitStack()
    k = Sched(nc, es)
    L, TOWN, D, DB, N, HD, R = (cfg.L, cfg.TOWN, cfg.D, cfg.DB, cfg.N,
                                cfg.HD, cfg.R)
    TW, WIN = cfg.TW, cfg.WIN
    DCH, NSL, NCH, PSL = cfg.DCH, cfg.NSL, cfg.NCH, cfg.PSL

    def din(name, shape, dt=F32):
        return nc.dram_tensor(name, list(shape), dt, kind="ExternalInput").ap()

    def dint(name, shape, dt):
        kind = "ExternalOutput" if dbg else "Internal"
        return nc.dram_tensor(name, list(shape), dt, kind=kind).ap()

    def sb(name, shape, dt=F32, scope=es):
        return scope.enter_context(nc.sbuf_tensor(name, list(shape), dt))

    def ps(name, shape, dt=F32, scope=es):
        return scope.enter_context(nc.psum_tensor(name, list(shape), dt))

    x_tok = din("x_tok", (L, D))
    x_win = din("x_win", (cfg.TW, D))
    xT_own = din("xT_own", (D, TOWN))
    W = {}
    for nm, sh in [("Wq", (D, D)), ("Wk", (D, 256)), ("Wv", (D, 256)),
                   ("Wo", (D, D)), ("Wx", (D, 320)), ("Wdt", (R, D)),
                   ("Wsel", (D, D)), ("Wimp", (D, N)), ("Wg", (D, D)),
                   ("Wout", (D, D)), ("W1", (D, cfg.F2)), ("W2", (cfg.F, D)),
                   ("wg4", (cfg.FI, D)), ("wu4", (cfg.FI, D)),
                   ("wd4", (D, cfg.FI))]:
        W[nm] = din(nm, sh)
    vec_in = {}
    for nm, sh in [("w_attn", (D, 1)), ("w_ssm", (D, 1)),
                   ("w_ffn", (D, 1)), ("qn_w", (HD, 1)), ("kn_w", (HD, 1)),
                   ("qn_rot", (HD, 1)), ("kn_rot", (HD, 1)), ("bdt", (D, 1)),
                   ("bsel", (D, 1)), ("selg", (D, 1)), ("bimp", (N, 1)),
                   ("bg", (D, 1)), ("Dp", (D, 1))]:
        vec_in[nm] = din(nm, sh)
    A_pairs_log = din("A_pairs_log", (128, cfg.OB * cfg.NB))
    cosK = din("cosK", (128, L)); sinK = din("sinK", (128, L))
    cosQ = din("cosQ", (128, TOWN)); sinQ = din("sinQ", (128, TOWN))
    cst_in = {}
    for nm, sh in [("ident", (128, 128)), ("rot", (128, 128)),
                   ("red2", (128, 2)), ("bcast2", (2, 128)),
                   ("r8st", (128, 1024)), ("onesrow", (1, 128)),
                   ("onescol", (128, 1))]:
        cst_in[nm] = din(nm, sh, BF16)
    bcast2f = din("bcast2f", (2, 128), F32)
    identf = din("identf", (128, 128), F32)
    outT = nc.dram_tensor("outT", [D, TOWN], F32, kind="ExternalOutput").ap()

    zT_sp = dint("zT_sp", (D, L), BF16)
    zTw_sp = dint("zTw_sp", (D, TW), BF16)
    xqT_sp = dint("xqT_sp", (D, TOWN), BF16)
    KT_sp = dint("KT_sp", (256, L), BF16)
    VT_sp = dint("VT_sp", (256, L), BF16)
    QT_sp = dint("QT_sp", (D, TOWN), BF16)
    aT_sp = dint("aT_sp", (D, TOWN), BF16)
    dTt_sp = dint("dTt_sp", (TW, D), BF16)
    dxTt_sp = dint("dxTt_sp", (TW, D), BF16)
    BT_sp = dint("BT_sp", (N, TW), BF16)
    CT_sp = dint("CT_sp", (N, TW), BF16)
    gT_sp = dint("gT_sp", (D, TOWN), BF16)
    y_own = dint("y_own", (D, TOWN), F32)
    wfold = {nm: dint(nm + "_bf", tuple(W[nm].shape), BF16)
             for nm in ("Wq", "Wk", "Wv", "Wsel", "Wx", "W1")}
    wq4 = {nm: dint(nm + "_qz", tuple(W[nm].shape), BF16)
           for nm in ("wg4", "wu4", "wd4")}

    C = {}
    evc = []
    for nm in cst_in:
        C[nm] = sb("c_" + nm, cst_in[nm].shape, BF16)
        evc.append(k.dma("sync", C[nm][:], cst_in[nm], ch="c"))
    c_bc2f = sb("c_bc2f", (2, 128))
    evc.append(k.dma("sync", c_bc2f[:], bcast2f, ch="c"))
    c_idf = sb("c_idf", (128, 128))
    evc.append(k.dma("sync", c_idf[:], identf, ch="c"))
    c_Apl = sb("c_Apl", (128, cfg.OB * cfg.NB))
    evc.append(k.dma("sync", c_Apl[:], A_pairs_log, ch="c"))
    c_w = sb("c_w", (128, 4 * DCH))
    for j in range(DCH):
        evc.append(k.dma("sync", c_w[:, 0 * DCH + j:0 * DCH + j + 1],
                         vec_in["w_attn"][j * 128:(j + 1) * 128, :], ch="c"))
        evc.append(k.dma("sync", c_w[:, 1 * DCH + j:1 * DCH + j + 1],
                         vec_in["w_ssm"][j * 128:(j + 1) * 128, :], ch="c"))
        evc.append(k.dma("sync", c_w[:, 2 * DCH + j:2 * DCH + j + 1],
                         vec_in["w_ffn"][j * 128:(j + 1) * 128, :], ch="c"))
        evc.append(k.dma("sync", c_w[:, 3 * DCH + j:3 * DCH + j + 1],
                         vec_in["w_attn"][j * 128:(j + 1) * 128, :], ch="c"))
    c_b = sb("c_b", (128, 6 * DCH + 6))
    off_bsel, off_selg = 0, DCH
    off_bg, off_bdt = 2 * DCH, 3 * DCH
    off_bimp = 4 * DCH
    off_Dp = off_bimp + 1
    off_qn = off_Dp + DCH
    off_nbg = off_qn + 4
    off_nbimp = off_nbg + DCH
    for j in range(DCH):
        evc.append(k.dma("sync", c_b[:, off_bsel + j:off_bsel + j + 1],
                         vec_in["bsel"][j * 128:(j + 1) * 128, :], ch="c"))
        evc.append(k.dma("sync", c_b[:, off_selg + j:off_selg + j + 1],
                         vec_in["selg"][j * 128:(j + 1) * 128, :], ch="c"))
        evc.append(k.dma("sync", c_b[:, off_Dp + j:off_Dp + j + 1],
                         vec_in["Dp"][j * 128:(j + 1) * 128, :], ch="c"))
        evc.append(k.dma("sync", c_b[:, off_bg + j:off_bg + j + 1],
                         vec_in["bg"][j * 128:(j + 1) * 128, :], ch="c"))
        evc.append(k.dma("sync", c_b[:, off_bdt + j:off_bdt + j + 1],
                         vec_in["bdt"][j * 128:(j + 1) * 128, :], ch="c"))
    evc.append(k.dma("sync", c_b[:, off_bimp:off_bimp + 1], vec_in["bimp"], ch="c"))
    for ci, nm in enumerate(["qn_w", "qn_rot", "kn_w", "kn_rot"]):
        evc.append(k.dma("sync", c_b[0:64, off_qn + ci:off_qn + ci + 1], vec_in[nm], ch="c"))
        evc.append(k.dma("sync", c_b[64:128, off_qn + ci:off_qn + ci + 1], vec_in[nm], ch="c"))
    c_sgb = sb("c_sgb", (128, DCH))
    c_Apairs = sb("c_Apairs", (128, cfg.OB * cfg.NB))
    c_Wdt = sb("c_Wdt", (R, D), BF16)
    evc.append(k.dma("gpsimd", c_Wdt[:], W["Wdt"], ch="cg"))

    ev_ct = ("c", k.cnt["c"])
    evc = [ev_ct, evc[-1]]
    ev_cl = evc[-1]
    k.op("vector", lambda e: e.tensor_scalar_mul(
        c_b[:, off_nbg:off_nbg + DCH], c_b[:, off_bg:off_bg + DCH], -1.0),
        waits=evc, incs=[("dve", 1)])
    ev_negb = k.op("vector", lambda e: e.tensor_scalar_mul(
        c_b[:, off_nbimp:off_nbimp + 1], c_b[:, off_bimp:off_bimp + 1], -1.0),
        waits=evc, incs=[("dve", 1)])
    c_nselg = sb("c_nselg", (128, DCH))
    e1a = k.op("vector", lambda e: e.tensor_scalar_mul(
        c_nselg[:], c_b[:, off_selg:off_selg + DCH], -1.0), waits=evc,
        incs=[("dve", 1)])
    e1 = k.op("vector", lambda e: e.tensor_tensor(
        c_sgb[:], c_nselg[:], c_b[:, off_bsel:off_bsel + DCH],
        ALU.mult), waits=[e1a], incs=[("dve", 1)])
    e2 = k.op("scalar", lambda e: e.activation(c_Apairs[:], c_Apl[:], AF.Exp),
              waits=evc, incs=[("act", 1)])
    e3 = k.op("vector", lambda e: e.tensor_scalar_mul(c_Apairs[:], c_Apairs[:], -1.0),
              waits=[e2], incs=[("dve", 1)])
    ev_setup = [e1, e3] + evc

    k.sgnmask_col()
    c_bias = sb("c_bias", (128, 1))
    c_bias_bf = sb("c_bias_bf", (1, 1), BF16)
    cvals = [cfg.EPS_LN, cfg.EPS_RMS, 1.0]
    c_fc = sb("c_fc", (128, len(cvals)))
    for i, v in enumerate(cvals):
        ev = k.op("vector", lambda e, i=i, v=v: e.memset(c_fc[:, i:i + 1], v),
                  incs=[("dve", 1)])
        nc.const_aps.register(F32, v, c_fc[:, i:i + 1])
    k.fence([ev])


    def finish(waits):
        ev = k.dma("sync", outT, xT_own, waits=list(waits), ch="dumo")
        k.op("gpsimd", lambda e: None, waits=[ev])
        blk = es.enter_context(nc.Block())
        for eng in ENGINES:
            def runner(e, _eng=eng):
                for f in k.ops[_eng]:
                    f(e)
            getattr(blk, eng)(runner)
        es.close()
        return nc

    s1 = contextlib.ExitStack()
    wf_st = [sb("wf_st%d" % i, (128, 1024), scope=s1) for i in range(2)]
    wf_ob = [sb("wf_ob%d" % i, (128, 1024), BF16, scope=s1) for i in range(2)]
    folds = [("Wq", 3 * DCH), ("Wk", 0), ("Wv", 0), ("Wsel", DCH), ("Wx", DCH),
             ("W1", 2 * DCH)]
    prev = [None, None]
    ib = 0
    ev_wfold = []
    for nm, wbase in folds:
        cols = W[nm].shape[1]
        for j in range(DCH):
            for c0 in range(0, cols, 1024):
                cw = min(1024, cols - c0)
                evl = k.dma("sync", wf_st[ib][:, 0:cw],
                            W[nm][j * 128:(j + 1) * 128, c0:c0 + cw],
                            waits=[prev[ib]], ch="wfst%d" % ib)
                evu = k.op("vector",
                           lambda e, ib=ib, cw=cw, wb=wbase, j=j:
                           e.tensor_scalar_mul(wf_ob[ib][:, 0:cw],
                                               wf_st[ib][:, 0:cw],
                                               c_w[:, wb + j:wb + j + 1]),
                           waits=[evl, ev_cl], incs=[("dve", 1)])
                prev[ib] = k.dma("gpsimd", wfold[nm][j * 128:(j + 1) * 128, c0:c0 + cw],
                                 wf_ob[ib][:, 0:cw], waits=[evu], ch="wfsp%d" % ib)
                ib ^= 1
        ev_wfold.append(prev[0])
        ev_wfold.append(prev[1])

    qb = QBufs(k, sb, s1, 2816)
    ev_wq4 = []
    qprev = None
    for nm in ("wg4", "wu4", "wd4"):
        rows, cols = W[nm].shape
        for r0 in range(0, rows, 128):
            evl = k.dma("sync", qb.st[:, 0:cols], W[nm][r0:r0 + 128, 0:cols],
                        waits=[qprev], ch="qbst")
            evq = emit_quant(k, qb, qb.st, qb.ob, cols, [evl])
            qprev = k.dma("gpsimd", wq4[nm][r0:r0 + 128, 0:cols],
                          qb.ob[:, 0:cols], waits=[evq], ch="wqsp")
            qb.guard = qprev
            ev_wq4.append(qprev)

    s2 = contextlib.ExitStack()
    s2x = [sb("s2x%d" % i, (128, D), scope=s2) for i in range(2)]
    s2st = sb("s2st", (128, 8), scope=s2)
    s2t = sb("s2t", (128, D), scope=s2)
    s2zb = sb("s2zb", (128, D), BF16, scope=s2)
    s2eb = sb("s2eb", (128, 128), BF16, scope=s2)
    p_tr = ps("p_tr", (128, 128), BF16, scope=s2)
    NSLW = cfg.TW // PSL
    ev_zT = [None] * NSL
    ev_zTw = [None] * NSLW
    ev_xq = [None]

    def ln_chunk(i, src_dram, zdst, do_quant, qdst, ev_prev, qoff=0):
        buf = s2x[i % 2]
        evx = k.dma("sync", buf[:], src_dram[i * 128:(i + 1) * 128, :],
                    waits=[ev_prev], ch="s2x%d" % (i % 2))
        e_mu = k.op("vector", lambda e: e.tensor_reduce(
            s2st[:, 0:1], buf[:], AX.X, ALU.add), waits=[evx], incs=[("dve", 1)])
        e_sq = k.op("scalar", lambda e: e.activation(
            s2t[:], buf[:], AF.Square), waits=[evx], incs=[("act", 1)])
        e_ss = k.op("vector", lambda e: e.tensor_reduce(
            s2st[:, 1:2], s2t[:], AX.X, ALU.add), waits=[e_sq, e_mu],
            incs=[("dve", 1)])
        e_st = k.op("vector", lambda e: e.tensor_scalar_mul(
            s2st[:, 0:2], s2st[:, 0:2], 1.0 / D), waits=[e_ss], incs=[("dve", 1)])
        e_v = k.op("vector", lambda e: e.scalar_tensor_tensor(
            s2st[:, 2:3], s2st[:, 0:1], -1.0, s2st[:, 0:1], ALU.mult, ALU.mult),
            waits=[e_st], incs=[("dve", 1)])
        e_v2 = k.op("vector", lambda e: e.tensor_tensor(
            s2st[:, 2:3], s2st[:, 1:2], s2st[:, 2:3], ALU.add),
            waits=[e_v], incs=[("dve", 1)])
        e_r1a = k.op("scalar", lambda e: e.activation(
            s2st[:, 3:4], s2st[:, 2:3], AF.Ln, bias=cfg.EPS_LN),
            waits=[e_v2], incs=[("act", 1)])
        e_r1 = k.op("scalar", lambda e: e.activation(
            s2st[:, 3:4], s2st[:, 3:4], AF.Exp, scale=-0.5),
            waits=[e_r1a], incs=[("act", 1)])
        e_q1 = k.op("vector", lambda e: e.scalar_tensor_tensor(
            s2st[:, 4:5], s2st[:, 3:4], 1.0, s2st[:, 3:4], ALU.mult, ALU.mult),
            waits=[e_r1], incs=[("dve", 1)])
        e_q2 = k.op("vector", lambda e: e.tensor_tensor(
            s2st[:, 4:5], s2st[:, 2:3], s2st[:, 4:5], ALU.mult),
            waits=[e_q1], incs=[("dve", 1)])
        e_r2a = k.op("scalar", lambda e: e.activation(
            s2st[:, 5:6], s2st[:, 4:5], AF.Ln, bias=cfg.EPS_RMS),
            waits=[e_q2], incs=[("act", 1)])
        e_r2 = k.op("scalar", lambda e: e.activation(
            s2st[:, 5:6], s2st[:, 5:6], AF.Exp, scale=-0.5),
            waits=[e_r2a], incs=[("act", 1)])
        e_s = k.op("vector", lambda e: e.tensor_tensor(
            s2st[:, 6:7], s2st[:, 3:4], s2st[:, 5:6], ALU.mult),
            waits=[e_r2], incs=[("dve", 1)])
        e_z = k.op("vector", lambda e: e.tensor_scalar(
            s2zb[:], buf[:], s2st[:, 0:1], s2st[:, 6:7], ALU.subtract, ALU.mult),
            waits=[e_s], incs=[("dve", 1)])
        ev_sp = None
        for j in range(DCH):
            e_t = k.op("tensor", lambda e, j=j: e.transpose(
                p_tr[:], s2zb[:, j * 128:(j + 1) * 128], C["ident"][:]),
                waits=[e_z, ev_sp] + (evc if i == 0 and j == 0 else []),
                incs=[("pe", 1)])
            e_ev = k.op("vector", lambda e: e.tensor_copy(s2eb[:], p_tr[:]),
                        waits=[e_t], incs=[("dve", 1)])
            ev_sp = k.dma("gpsimd", zdst[j * 128:(j + 1) * 128, i * 128:(i + 1) * 128],
                          s2eb[:], waits=[e_ev], ch="spz")
        ev_q = None
        if do_quant:
            e_xn = k.op("vector", lambda e: e.tensor_scalar(
                s2t[:], buf[:], s2st[:, 0:1], s2st[:, 3:4], ALU.subtract, ALU.mult),
                waits=[e_s], incs=[("dve", 1)])
            evq = emit_quant(k, qb, s2t, qb.ob, D, [e_xn])
            c0 = i * 128 - qoff
            for j in range(DCH):
                e_t = k.op("tensor", lambda e, j=j: e.transpose(
                    p_tr[:], qb.ob[:, j * 128:(j + 1) * 128], C["ident"][:]),
                    waits=[evq, ev_sp, ev_q], incs=[("pe", 1)])
                e_ev = k.op("vector", lambda e: e.tensor_copy(s2eb[:], p_tr[:]),
                            waits=[e_t], incs=[("dve", 1)])
                ev_q = k.dma("gpsimd", qdst[j * 128:(j + 1) * 128, c0:c0 + 128],
                             s2eb[:], waits=[e_ev], ch="spz")
                qb.guard = e_t
        return ev_sp, ev_q

    prev_sp = None
    for i in range(NCH):
        prev_sp, _ = ln_chunk(i, x_tok, zT_sp, False, None, prev_sp)
        sl = ((i + 1) * 128) // PSL
        if (i + 1) * 128 % PSL == 0:
            ev_zT[sl - 1] = prev_sp
    for i in range(cfg.TW // 128):
        doq = i * 128 >= WIN
        prev_sp, evq = ln_chunk(i, x_win, zTw_sp, doq,
                                xqT_sp if doq else None, prev_sp, qoff=WIN)
        if doq and evq is not None:
            ev_xq[0] = evq
        sl = ((i + 1) * 128) // PSL
        if (i + 1) * 128 % PSL == 0:
            ev_zTw[sl - 1] = prev_sp

    s2.close()
    s1.close()
    k.fence([prev_sp] + [e for e in ev_xq if e] + ev_wfold + ev_wq4 +
            [ev_zT[-1], ev_zTw[-1]])
    if upto <= 2:
        return finish([])

    s3 = contextlib.ExitStack()
    zsl = [sb("zsl%d" % j, (128, PSL), BF16, scope=s3) for j in range(DCH)]
    wst = [sb("wst%d" % i, (128, 128), BF16, scope=s3) for i in range(16)]
    xsl = [sb("xsl%d" % j, (128, PSL), BF16, scope=s3) for j in range(DCH)]
    s3a = sb("s3a", (128, PSL), scope=s3)
    s3b = sb("s3b", (128, PSL), scope=s3)
    s3c = sb("s3c", (128, PSL), scope=s3)
    s3bf = sb("s3bf", (128, PSL), BF16, scope=s3)
    s3bf2 = sb("s3bf2", (128, PSL), BF16, scope=s3)
    s3bf3 = sb("s3bf3", (128, PSL), BF16, scope=s3)
    s3r = sb("s3r", (2, PSL), scope=s3)
    s3rb = sb("s3rb", (2, PSL), BF16, scope=s3)
    s3cos = sb("s3cos", (128, PSL), scope=s3)
    s3sin = sb("s3sin", (128, PSL), scope=s3)
    s3wc = sb("s3wc", (128, PSL), scope=s3)
    s3ws = sb("s3ws", (128, PSL), scope=s3)
    s3mx = sb("s3mx", (128, 4), scope=s3)
    pp = [ps("pp%d" % i, (128, PSL), scope=s3) for i in range(2)]
    p_sm = ps("p_sm", (2, PSL), scope=s3)
    p_rep = ps("p_rep", (128, PSL), scope=s3)
    p_trd = ps("p_trd", (128, 128), BF16, scope=s3)
    tr_eb = sb("tr_eb", (128, 128), BF16, scope=s3)
    guard = {}

    def spill_T(src, dst, tbase, wait_ev, j):
        ev = e_t = None
        for q in range(PSL // 128):
            e_t = k.op("tensor", lambda e, q=q, src=src: e.transpose(
                p_trd[:], src[:, q * 128:(q + 1) * 128], C["ident"][:]),
                waits=[wait_ev, guard.get("p_trd")], incs=[("pe", 1)])
            e_ev = k.op("vector", lambda e: e.tensor_copy(tr_eb[:], p_trd[:]),
                        waits=[e_t, guard.get("tr_eb")], incs=[("dve", 1)])
            guard["p_trd"] = e_ev
            ev = k.dma("gpsimd", dst[tbase + q * 128:tbase + (q + 1) * 128,
                                     j * 128:(j + 1) * 128], tr_eb[:],
                       waits=[e_ev], ch="sps3")
            guard["tr_eb"] = ev
        return ev, e_t

    s3ctr = [0]
    e0 = k.op("vector", lambda e: e.memset(s3mx[:], 0.0), incs=[("dve", 1)])
    guard["mx"] = e0

    wseq = [0]
    gseq = [0]

    def mm_group(pbuf, w_dram, col0, ncols, rhs_tiles, nk, waits=(), rhs_rows=128):
        used, evw = [], None
        q = "sync" if w_dram.dtype == BF16 else "gpsimd"
        wch = "wstl%d%s" % (gseq[0] & 1, q[0])
        gseq[0] += 1
        for j in range(nk):
            sti = wseq[0] % 16
            wseq[0] += 1
            used.append(sti)
            evw = k.dma(q, wst[sti][0:128, 0:ncols],
                        w_dram[j * 128:(j + 1) * 128, col0:col0 + ncols],
                        waits=[guard.get(("wst", sti))], ch=wch)
        ev = None
        for j, sti in enumerate(used):
            ev = k.op("tensor",
                      lambda e, j=j, sti=sti, last=(j == nk - 1), first=(j == 0):
                      e.matmul(pbuf[0:ncols, :], wst[sti][0:128, 0:ncols],
                               rhs_tiles[j], start=first, stop=last),
                      waits=[evw] + list(waits) + ([guard.get(id(pbuf))] if j == 0 else []),
                      incs=[("pe", 1)] if j == nk - 1 else ())
        for sti in used:
            guard[("wst", sti)] = ev
        return ev

    def load_zslice(s, src, evs, off=0):
        out = []
        for j in range(DCH):
            out.append(k.dma("scalar", zsl[j][:],
                             src[j * 128:(j + 1) * 128,
                                 off + s * PSL:off + (s + 1) * PSL],
                             waits=[evs, guard.get(("zsl", j)),
                                    guard.get("zslT")], ch="zsll"))
        return out

    def rms_rope(pbuf, wcol_qn, cosbuf, sinbuf, ev_mm, tag, ev_tab=(),
                 rlevel=3):
        e_sq = k.op("scalar", lambda e: e.activation(s3bf[:], pbuf[:], AF.Square),
                    waits=[ev_mm, guard.get("s3bf")], incs=[("act", 1)])
        if rlevel == -1:
            e_o = k.op("vector", lambda e: e.tensor_copy(s3bf3[:], s3bf[:]),
                       waits=[e_sq, guard.get("s3bf3")], incs=[("dve", 1)])
            guard["s3bf"] = e_o
            guard[id(pbuf)] = e_o
            return e_o, e_o
        e_rm = k.op("tensor", lambda e: e.matmul(p_sm[:], C["red2"][:], s3bf[:],
                                                 start=True, stop=True),
                    waits=[e_sq, guard.get(id(p_sm))], incs=[("pe", 1)])
        e_l = k.op("scalar", lambda e: e.activation(s3r[:], p_sm[:], AF.Ln,
                                                    scale=1.0 / 64, bias=cfg.EPS_RMS),
                   waits=[e_rm], incs=[("act", 1)])
        e_e = k.op("scalar", lambda e: e.activation(s3rb[:], s3r[:], AF.Exp,
                                                    scale=-0.5),
                   waits=[e_l, guard.get("s3rb")], incs=[("act", 1)])
        guard[id(p_sm)] = e_e
        if rlevel == -2:
            e_o = k.op("vector", lambda e: e.tensor_copy(s3bf3[0:2, :], s3rb[:]),
                       waits=[e_e, guard.get("s3bf3")], incs=[("dve", 1)])
            e_o = k.op("vector", lambda e: e.tensor_copy(s3bf3[:], pbuf[:]),
                       waits=[e_o, ev_mm], incs=[("dve", 1)])
            guard["s3bf"] = e_rm
            guard[id(pbuf)] = e_o
            return e_o, e_o
        if rlevel in (-10, -11, -12):
            w10 = [e_e, ev_mm] if rlevel == -10 else [ev_mm]
            rdbuf = pp[1] if rlevel == -12 else pbuf
            e_o = k.op("vector", lambda e, rdbuf=rdbuf: e.tensor_copy(
                s3bf3[:], rdbuf[:]),
                waits=w10 + [guard.get("s3bf3")], incs=[("dve", 1)])
            guard["s3bf"] = e_rm
            guard["s3rb"] = e_e
            guard[id(pbuf)] = e_o
            return e_o, e_o
        if rlevel == -5:
            e_ev = k.op("vector", lambda e: e.tensor_copy(s3bf2[:], pbuf[:]),
                        waits=[ev_mm, guard.get("s3bf2")], incs=[("dve", 1)])
            guard[id(pbuf)] = e_ev
            e_o = k.op("vector", lambda e: e.tensor_copy(s3bf3[:], s3bf2[:]),
                       waits=[e_ev, guard.get("s3bf3")], incs=[("dve", 1)])
            guard["s3bf"] = e_rm
            guard["s3rb"] = e_e
            guard["s3bf2"] = e_o
            return e_o, e_o
        e_rep = k.op("tensor", lambda e: e.matmul(p_rep[:], C["bcast2"][:],
                                                  s3rb[:],
                                                  start=True, stop=True),
                     waits=[e_e, guard.get(id(p_rep))], incs=[("pe", 1)])
        guard["s3rb"] = e_rep
        if rlevel == -4:
            e_o = k.op("vector", lambda e: e.tensor_copy(s3bf3[:], p_rep[:]),
                       waits=[e_rep, guard.get("s3bf3")], incs=[("dve", 1)])
            guard["s3bf"] = e_rm
            guard[id(p_rep)] = e_o
            guard[id(pbuf)] = ev_mm
            return e_o, e_o
        e_ev = k.op("vector", lambda e: e.tensor_copy(s3a[:], pbuf[:]),
                    waits=[ev_mm, e_sq, guard.get("s3a")], incs=[("dve", 1)])
        guard[id(pbuf)] = e_ev
        if rlevel == -3:
            e_o = k.op("vector", lambda e: e.tensor_copy(s3bf3[:], p_rep[:]),
                       waits=[e_rep, e_ev, guard.get("s3bf3")], incs=[("dve", 1)])
            guard["s3bf"] = e_rm
            guard["s3a"] = e_o
            guard[id(p_rep)] = e_o
            return e_o, e_o
        e_krms = k.op("vector", lambda e: e.tensor_tensor(s3bf2[:], s3a[:],
                                                          p_rep[:], ALU.mult),
                      waits=[e_ev, e_rep, guard.get("s3bf2")], incs=[("dve", 1)])
        guard["s3a"] = e_krms
        guard[id(p_rep)] = e_krms
        if rlevel == 1:
            e_o = k.op("vector", lambda e: e.tensor_copy(s3bf3[:], s3bf2[:]),
                       waits=[e_krms, guard.get("s3bf3")], incs=[("dve", 1)])
            guard["s3bf2"] = e_o
            return e_o, e_o
        e_wc = k.op("vector", lambda e: e.tensor_scalar_mul(
            s3wc[:], cosbuf[:], c_b[:, off_qn + wcol_qn:off_qn + wcol_qn + 1]),
            waits=[guard.get("s3wc")] + list(ev_tab), incs=[("dve", 1)])
        e_ws = k.op("vector", lambda e: e.tensor_scalar_mul(
            s3ws[:], sinbuf[:], c_b[:, off_qn + wcol_qn + 1:off_qn + wcol_qn + 2]),
            waits=[guard.get("s3ws")] + list(ev_tab), incs=[("dve", 1)])
        e_rot = k.op("tensor", lambda e: e.matmul(p_rep[:], C["rot"][:], s3bf2[:],
                                                  start=True, stop=True),
                     waits=[e_krms], incs=[("pe", 1)])
        e_t1 = k.op("vector", lambda e: e.tensor_tensor(s3b[:], s3bf2[:], s3wc[:],
                                                        ALU.mult),
                    waits=[e_wc, e_krms, guard.get("s3b")], incs=[("dve", 1)])
        guard["s3bf2"] = e_t1
        guard["s3wc"] = e_t1
        e_t2 = k.op("vector", lambda e: e.tensor_tensor(s3c[:], p_rep[:], s3ws[:],
                                                        ALU.mult),
                    waits=[e_ws, e_rot, guard.get("s3c")], incs=[("dve", 1)])
        guard[id(p_rep)] = e_t2
        guard["s3ws"] = e_t2
        e_o = k.op("vector", lambda e: e.tensor_tensor(s3bf3[:], s3b[:], s3c[:],
                                                       ALU.add),
                   waits=[e_t1, e_t2, guard.get("s3bf3")], incs=[("dve", 1)])
        guard["s3b"] = e_o
        guard["s3c"] = e_o
        if rlevel == 2:
            return e_o, e_o
        e_mx = k.op("vector", lambda e: e.tensor_reduce(
            s3mx[:, 2:3], s3bf3[:], AX.X, ALU.max, apply_absolute_value=True),
            waits=[e_o, guard.get("mx")], incs=[("dve", 1)])
        mcol = 0 if wcol_qn == 0 else 1
        e_mx2 = k.op("vector", lambda e, mcol=mcol: e.tensor_tensor(
            s3mx[:, mcol:mcol + 1], s3mx[:, mcol:mcol + 1], s3mx[:, 2:3], ALU.max),
            waits=[e_mx], incs=[("dve", 1)])
        guard["mx"] = e_mx2
        guard["s3bf"] = e_rm
        return e_o, e_mx2

    for s in range(NSL if upto != 30 else 0):
        evz = load_zslice(s, zT_sp, ev_zT[s])
        tsl = slice(s * PSL, (s + 1) * PSL)
        if upto == 311:
            pass
        elif upto == 312:
            for kc in range(2):
                e_mm = mm_group(pp[0], wfold["Wk"], kc * 128, 128,
                                [z[:] for z in zsl], DCH, waits=evz + ev_wfold)
                e_ev = k.op("vector", lambda e: e.tensor_copy(s3bf3[:], pp[0][:]),
                            waits=[e_mm, guard.get("s3bf3")], incs=[("dve", 1)])
                guard[id(pp[0])] = e_ev
                ev_sp = k.dma("gpsimd", KT_sp[kc * 128:(kc + 1) * 128, tsl],
                              s3bf3[:], waits=[e_ev], ch="sps3")
                guard["s3bf3"] = ev_sp
        else:
            e_cos = k.dma("scalar", s3cos[:], cosK[:, tsl],
                          waits=[guard.get("cosld")], ch="cosl")
            e_sin = k.dma("scalar", s3sin[:], sinK[:, tsl],
                          waits=[guard.get("cosld")], ch="cosl")
            for kc in range(2):
                limited = 3200 < upto < 3300
                if limited and s3ctr[0] >= upto - 3200:
                    break
                s3ctr[0] += 1
                e_mm = mm_group(pp[0], wfold["Wk"], kc * 128, 128,
                                [z[:] for z in zsl], DCH, waits=evz + ev_wfold)
                e_o, _ = rms_rope(pp[0], 2, s3cos, s3sin, e_mm, "k",
                                  ev_tab=[e_cos, e_sin],
                                  rlevel={313: 1, 314: 2, 3131: -1, 3132: -2, 3133: -3, 3134: -4, 3135: -5, 3110: -10, 3111: -11, 3112: -12}.get(upto, 3))
                ev_sp = k.dma("gpsimd", KT_sp[kc * 128:(kc + 1) * 128, tsl],
                              s3bf3[:], waits=[e_o], ch="sps3")
                guard["s3bf3"] = ev_sp
            guard["cosld"] = guard.get("s3ws")
        for kc in range(2):
            e_mm = mm_group(pp[0], wfold["Wv"], kc * 128, 128, [z[:] for z in zsl],
                            DCH, waits=evz)
            e_ev = k.op("vector", lambda e: e.tensor_copy(s3bf[:], pp[0][:]),
                        waits=[e_mm, guard.get("s3bf")], incs=[("dve", 1)])
            guard[id(pp[0])] = e_ev
            ev_sp = k.dma("gpsimd", VT_sp[kc * 128:(kc + 1) * 128, tsl], s3bf[:],
                          waits=[e_ev], ch="sps3")
            guard["s3bf"] = ev_sp
        guard["zslT"] = e_mm

    if upto in (30, 31, 311, 312, 313, 314, 3131, 3132, 3133, 3134, 3135) or 3200 < upto < 3300 or upto in (3110, 3111, 3112):
        k.fence([guard.get("s3bf"), guard.get("s3bf3")])
        s3.close()
        return finish([])
    for s in range(NSLW):
        evz = load_zslice(s, zTw_sp, ev_zTw[s])
        tsl = slice(s * PSL, (s + 1) * PSL)
        xs_evs = []
        for j in range(DCH):
            e_mm = mm_group(pp[1], wfold["Wsel"], j * 128, 128, [z[:] for z in zsl],
                            DCH, waits=evz)
            e_s1 = k.op("scalar", lambda e, j=j: e.activation(
                s3a[:], pp[1][:], AF.Exp, scale=c_nselg[:, j:j + 1],
                bias=c_sgb[:, j:j + 1]),
                waits=[e_mm, ev_setup[0], guard.get("s3a")], incs=[("act", 1)])
            e_s2 = k.op("scalar", lambda e: e.activation(
                s3a[:], s3a[:], AF.Ln, bias=1.0), waits=[e_s1], incs=[("act", 1)])
            e_sg = k.op("scalar", lambda e: e.activation(
                s3bf[:], s3a[:], AF.Exp, scale=-1.0),
                waits=[e_s2, guard.get("s3bf")], incs=[("act", 1)])
            guard["s3a"] = e_sg
            guard[id(pp[1])] = e_s1
            e_xs = k.op("vector", lambda e, j=j: e.scalar_tensor_tensor(
                xsl[j][:], zsl[j][:], c_w[:, DCH + j:DCH + j + 1], s3bf[:],
                ALU.mult, ALU.mult),
                waits=[e_sg, guard.get(("xsl", j))], incs=[("dve", 1)])
            guard["s3bf"] = e_xs
            guard[("zsl", j)] = e_xs
            guard[("xsl", j)] = e_xs
            xs_evs.append(e_xs)
        e_mm = mm_group(pp[0], W["Wimp"], 0, N, [x[:] for x in xsl], DCH,
                        waits=xs_evs)
        e_i1 = k.op("scalar", lambda e: e.activation(
            s3a[:], pp[0][:], AF.Exp, scale=-1.0,
            bias=c_b[:, off_nbimp:off_nbimp + 1]),
            waits=[e_mm, guard.get("s3a")], incs=[("act", 1)])
        e_i2 = k.op("scalar", lambda e: e.activation(
            s3a[:], s3a[:], AF.Ln, bias=1.0), waits=[e_i1], incs=[("act", 1)])
        e_imp = k.op("scalar", lambda e: e.activation(
            s3bf[:], s3a[:], AF.Exp, scale=-1.0),
            waits=[e_i2, guard.get("s3bf")], incs=[("act", 1)])
        guard["s3a"] = e_imp
        guard[id(pp[0])] = e_i1
        e_dr = mm_group(pp[1], wfold["Wx"], 0, 64, [z[:] for z in zsl], DCH,
                        waits=evz)
        e_drc = k.op("vector", lambda e: e.tensor_copy(s3bf2[0:64, :], pp[1][0:64, :]),
                     waits=[e_dr, guard.get("s3bf2")], incs=[("dve", 1)])
        guard[id(pp[1])] = e_drc
        for ci, (c0, dst) in enumerate([(64, BT_sp), (192, CT_sp)]):
            e_mm2 = mm_group(pp[0] if ci == 0 else pp[1], wfold["Wx"], c0, 128,
                             [z[:] for z in zsl], DCH)
            pbuf = pp[0] if ci == 0 else pp[1]
            e_bt = k.op("vector", lambda e, pbuf=pbuf: e.tensor_tensor(
                s3bf3[:], pbuf[:], s3bf[:], ALU.mult),
                waits=[e_mm2, e_imp, guard.get("s3bf3")], incs=[("dve", 1)])
            guard[id(pbuf)] = e_bt
            ev_sp = k.dma("gpsimd", dst[:, tsl], s3bf3[:], waits=[e_bt], ch="sps3")
            guard["s3bf3"] = ev_sp
        guard["s3bf"] = ev_sp
        guard["zslT"] = e_mm2
        for j in range(DCH):
            e_dm = k.op("tensor", lambda e, j=j: e.matmul(
                pp[0][:], c_Wdt[:, j * 128:(j + 1) * 128], s3bf2[0:64, :],
                start=True, stop=True),
                waits=[e_drc, guard.get(id(pp[0])), evc[-1]], incs=[("pe", 1)])
            e_ab = k.op("scalar", lambda e, j=j: e.activation(
                s3a[:], pp[0][:], AF.Abs, bias=c_b[:, off_bdt + j:off_bdt + j + 1]),
                waits=[e_dm, guard.get("s3a")], incs=[("act", 1)])
            e_rl = k.op("scalar", lambda e, j=j: e.activation(
                s3b[:], pp[0][:], AF.Relu, bias=c_b[:, off_bdt + j:off_bdt + j + 1]),
                waits=[e_dm, guard.get("s3b")], incs=[("act", 1)])
            guard[id(pp[0])] = e_rl
            e_ex = k.op("scalar", lambda e: e.activation(
                s3a[:], s3a[:], AF.Exp, scale=-1.0), waits=[e_ab], incs=[("act", 1)])
            e_ln = k.op("scalar", lambda e: e.activation(
                s3a[:], s3a[:], AF.Ln, bias=1.0), waits=[e_ex], incs=[("act", 1)])
            e_dl = k.op("vector", lambda e: e.tensor_tensor(
                s3c[:], s3b[:], s3a[:], ALU.add),
                waits=[e_ln, e_rl, guard.get("s3c")], incs=[("dve", 1)])
            guard["s3a"] = e_dl
            guard["s3b"] = e_dl
            e_dbf = k.op("vector", lambda e: e.tensor_copy(s3bf3[:], s3c[:]),
                         waits=[e_dl, guard.get("s3bf3")], incs=[("dve", 1)])
            ev_spd, e_rdT = spill_T(s3bf3, dTt_sp, s * PSL, e_dbf, j)
            guard["s3bf3"] = e_rdT
            e_gm = mm_group(pp[1], W["Wg"], j * 128, 128, [x[:] for x in xsl],
                            DCH, waits=xs_evs)
            e_g1 = k.op("scalar", lambda e, j=j: e.activation(
                s3a[:], pp[1][:], AF.Exp, scale=-1.0,
                bias=c_b[:, off_nbg + j:off_nbg + j + 1]),
                waits=[e_gm, guard.get("s3a")], incs=[("act", 1)])
            e_g2 = k.op("scalar", lambda e: e.activation(
                s3a[:], s3a[:], AF.Ln, bias=1.0), waits=[e_g1], incs=[("act", 1)])
            e_gs = k.op("scalar", lambda e: e.activation(
                s3bf[:], s3a[:], AF.Exp, scale=-1.0),
                waits=[e_g2, guard.get("s3bf")], incs=[("act", 1)])
            guard["s3a"] = e_gs
            guard[id(pp[1])] = e_g1
            e_xg = k.op("vector", lambda e, j=j: e.tensor_tensor(
                s3b[:], xsl[j][:], s3bf[:], ALU.mult),
                waits=[e_gs, guard.get("s3b")], incs=[("dve", 1)])
            guard["s3bf"] = e_xg
            e_dx = k.op("vector", lambda e: e.tensor_tensor(
                s3bf3[:], s3c[:], s3b[:], ALU.mult),
                waits=[e_xg, e_dl, guard.get("s3bf3")], incs=[("dve", 1)])
            guard["s3c"] = e_dx
            guard["s3b"] = e_dx
            ev_spx, e_rxT = spill_T(s3bf3, dxTt_sp, s * PSL, e_dx, j)
            guard["s3bf3"] = e_rxT
    ev_s3_full = [ev_spd, ev_spx, ev_sp]

    if upto == 32:
        k.fence(ev_s3_full)
        s3.close()
        return finish([])
    for s in range(TOWN // PSL):
        evz = load_zslice(s, zTw_sp, ev_zTw[(WIN + (s + 1) * PSL) // PSL - 1],
                          off=WIN)
        tsl = slice(s * PSL, (s + 1) * PSL)
        e_cos = k.dma("scalar", s3cos[:], cosQ[:, tsl], waits=[guard.get("cosld")],
                      ch="cosl")
        e_sin = k.dma("scalar", s3sin[:], sinQ[:, tsl], waits=[guard.get("cosld")],
                      ch="cosl")
        for kc in range(DCH):
            e_mm = mm_group(pp[0], wfold["Wq"], kc * 128, 128, [z[:] for z in zsl],
                            DCH, waits=evz + [e_cos, e_sin])
            e_o, _ = rms_rope(pp[0], 0, s3cos, s3sin, e_mm, "q",
                              ev_tab=[e_cos, e_sin])
            ev_sp = k.dma("gpsimd", QT_sp[kc * 128:(kc + 1) * 128, tsl], s3bf3[:],
                          waits=[e_o], ch="sps3")
            guard["s3bf3"] = ev_sp
        guard["cosld"] = guard.get("s3ws")
        guard["zslT"] = e_mm
    ev_s3_q = [ev_sp]

    if upto == 33:
        k.fence(ev_s3_full + ev_s3_q)
        s3.close()
        return finish([])
    bnd_sp = dint("bnd_sp", (128, 1), F32)
    bnd2_sp = dint("bnd2_sp", (1, 1), F32)
    e_mg = k.op("vector", lambda e: e.tensor_tensor(
        s3mx[:, 0:1], s3mx[:, 0:1], s3mx[:, 1:2], ALU.max),
        waits=[guard["mx"]] + ev_s3_q + ev_s3_full, incs=[("dve", 1)])
    ev_b1 = k.dma("sync", bnd_sp, s3mx[:, 0:1], waits=[e_mg], ch="bnd")
    ev_b2 = k.dma("sync", s3cos[0:1, 0:128],
                  bnd_sp.rearrange("p o -> o p"),
                  waits=[ev_b1, guard.get("s3wc")], ch="bnd")
    e_mr = k.op("vector", lambda e: e.tensor_reduce(
        s3mx[0:1, 3:4], s3cos[0:1, 0:128], AX.X, ALU.max), waits=[ev_b2],
        incs=[("dve", 1)])
    e_pr = k.op("vector", lambda e: e.tensor_tensor(
        s3mx[0:1, 2:3], s3mx[0:1, 3:4], s3mx[0:1, 3:4], ALU.mult),
        waits=[e_mr], incs=[("dve", 1)])
    e_sc = k.op("vector", lambda e: e.tensor_scalar_mul(
        s3mx[0:1, 2:3], s3mx[0:1, 2:3], -0.125), waits=[e_pr], incs=[("dve", 1)])
    ev_b3 = k.dma("sync", bnd2_sp, s3mx[0:1, 2:3], waits=[e_sc], ch="bnd")
    ev_bias = k.dma("sync", c_bias[:],
                    bnd2_sp.unsqueeze(0).broadcast_to((128, 1, 1)),
                    waits=[ev_b3], ch="bnd")

    s3.close()
    k.fence(ev_s3_full + ev_s3_q + [ev_bias] + ev_wq4 + [ev_xq[-1]])
    if upto <= 3:
        return finish([])
    s45 = contextlib.ExitStack()
    KTN = L // 128
    QC = cfg.QC
    QCN = TOWN // QC
    SCT, TPE = cfg.SCT, cfg.TPE
    SWIN = 128
    SCAN0 = WIN - SWIN
    NSC = (TW - SCAN0) // SCT
    CPB = TPE // SCT
    KSC_W0 = (WIN - SCAN0) // SCT
    NBLK = TOWN // TPE

    uctr = [0]
    kt_t = sb("kt_t", (64, L), BF16, scope=s45)
    VAW = 128
    vaug = sb("vaug", (128, KTN * VAW), BF16, scope=s45)
    qgb = [sb("qgb%d" % i, (64, QC), BF16, scope=s45) for i in range(2)]
    pP = [sb("pP%d" % i, (128, QC), BF16, scope=s45) for i in range(2)]
    s4r = sb("s4r", (1, QC), BF16, scope=s45)
    s4rb = sb("s4rb", (1, QC), BF16, scope=s45)
    s4rep = sb("s4rep", (64, QC), BF16, scope=s45)
    rcp_sp = dint("rcp_sp", (1, QC), BF16)
    pv_dbg = dint("pv_dbg", (65, QC), BF16) if dbg else None
    ps_dbg = dint("ps_dbg", (64, QC), BF16) if dbg else None
    va_dbg = dint("va_dbg", (128, KTN * VAW), BF16) if dbg else None
    pp_dbg = dint("pp_dbg", (64, QC), BF16) if dbg else None
    s4o = sb("s4o", (64, QC), BF16, scope=s45)
    pS = [ps("pS%d" % i, (128, QC), scope=s45) for i in range(2)]
    pPV = [ps("pPV%d" % i, (65, QC // 2), scope=s45) for i in range(2)]

    BSB = 8
    dstg = [sb("dstg%d" % i, (128, SCT * 256), BF16, scope=s45) for i in range(2)]
    xstg = [sb("xstg%d" % i, (128, SCT * 256), BF16, scope=s45) for i in range(2)]
    bstg = [sb("bstg%d" % i, (128, 4 * BSB * SCT), BF16, scope=s45)
            for i in range(2)]
    cstg = [sb("cstg%d" % i, (128, 4 * BSB * SCT), BF16, scope=s45)
            for i in range(2)]
    lbuf = [[sb("lbuf%d_%d" % (e2, i), (128, SCT * 512), BF16, scope=s45)
             for i in range(2)] for e2 in range(2)]
    abuf = [[sb("abuf%d_%d" % (e2, i), (128, SCT * 512), BF16, scope=s45)
             for i in range(2)] for e2 in range(2)]
    hsv = [[sb("hsv%d_%d" % (e2, i), (128, SCT * 512), BF16, scope=s45)
            for i in range(2)] for e2 in range(2)]
    hmt = [sb("hmt%d" % e2, (128, 512), BF16, scope=s45) for e2 in range(2)]
    wringb = [sb("wringb%d" % e2, (128, 128 * TPE), BF16, scope=s45)
              for e2 in range(2)]
    ystg = [sb("ystg%d" % e2, (128, TPE), scope=s45) for e2 in range(2)]
    pY = [ps("pY%d" % i, (128, TPE), scope=s45) for i in range(2)]
    SENG = [("vector", "dve"), ("gpsimd", "pol")]

    def attn_unit(kvh, hg, qc, first_of_kv):
        evs = []
        if first_of_kv:
            evk = k.dma("scalar", kt_t[:], KT_sp[kvh * 64:(kvh + 1) * 64, :],
                        waits=[guard.get("kt_t")], ch="ktld")
            guard["kt_t"] = evk
            evm = k.op("vector", lambda e: e.memset(vaug[:], 1.0),
                       waits=[guard.get("vaug")], incs=[("dve", 1)])
            evv = None
            for kt in range(KTN):
                evv = k.dmat("sync", vaug[:, kt * VAW:kt * VAW + 64],
                             VT_sp[kvh * 64:(kvh + 1) * 64, kt * 128:(kt + 1) * 128],
                             waits=[evm], ch="vgld")
            guard["vaug"] = evv
            guard["kvload"] = (evk, evv)
        evk, evv = guard["kvload"]
        ub = uctr[0] % 2
        uctr[0] += 1
        qt = qgb[ub]
        evq = k.dma("scalar", qt[:],
                    QT_sp[kvh * 256 + hg * 64:kvh * 256 + (hg + 1) * 64,
                          qc * QC:(qc + 1) * QC],
                    waits=[guard.get(("qgb", ub))], ch="qgld%d" % ub)
        e_pv = None
        HQC = QC // 2
        for kt in range(KTN):
            ib = kt % 2
            e_s = None
            for hh in range(2):
                e_s = k.op("tensor", lambda e, kt=kt, ib=ib, qt=qt, hh=hh:
                           e.matmul(pS[ib][:, hh * HQC:(hh + 1) * HQC],
                                    kt_t[:, kt * 128:(kt + 1) * 128],
                                    qt[:, hh * HQC:(hh + 1) * HQC],
                                    start=True, stop=True),
                           waits=[evk, evq, guard.get(("pS", ib)),
                                  guard.get(("pPfree", ib))],
                           incs=[("pe", 1)] if hh == 1 else ())
            e_p = k.op("scalar", lambda e, ib=ib: e.activation(
                pP[ib][:], pS[ib][:], AF.Exp, scale=0.125, bias=c_bias[:, 0:1]),
                waits=[e_s, ev_bias], incs=[("act", 1)])
            guard[("pS", ib)] = e_p
            if dbg and kvh == 3 and hg == 1 and qc == 0 and kt == 0:
                e_d1 = k.op("vector", lambda e, ib=ib: e.tensor_copy(
                    s4o[:], pS[ib][0:64, 0:QC]), waits=[e_p, guard.get("s4o")],
                    incs=[("dve", 1)])
                guard["s4o"] = k.dma("sync", ps_dbg, s4o[:],
                                     waits=[e_d1], ch="dbg")
                guard["pP_dbg"] = k.dma("sync", pp_dbg,
                                        pP[ib][0:64, 0:QC],
                                        waits=[e_p], ch="dbg")
            for hh in range(2):
                e_pv = k.op("tensor", lambda e, kt=kt, ib=ib, hh=hh: e.matmul(
                    pPV[hh][:],
                    vaug[:, kt * VAW:kt * VAW + 65],
                    pP[ib][:, hh * HQC:(hh + 1) * HQC],
                    start=(kt == 0), stop=(kt == KTN - 1)),
                    waits=[e_p, evv] + ([guard.get("pPV")] if kt == 0 else []),
                    incs=[("pe", 1)] if (kt == KTN - 1 and hh == 1) else ())
            guard[("pPfree", ib)] = e_pv if e_pv else guard.get(("pPfree", ib))
            tick()
        if dbg and kvh == 3 and hg == 1 and qc == 0:
            guard["va_dbg"] = k.dma("sync", va_dbg, vaug[:], waits=[e_pv],
                                    ch="dbg")
            for hh in range(2):
                e_dg = k.op("vector", lambda e, hh=hh: e.tensor_copy(
                    s4o[:, hh * HQC:(hh + 1) * HQC], pPV[hh][0:64, :]),
                    waits=[e_pv, guard.get("s4o")], incs=[("dve", 1)])
                e_dg2 = k.op("vector", lambda e, hh=hh: e.tensor_copy(
                    s4rb[:, hh * HQC:(hh + 1) * HQC], pPV[hh][64:65, :]),
                    waits=[e_pv, guard.get("s4rb")], incs=[("dve", 1)])
                guard["s4o"] = k.dma("sync", pv_dbg[0:64, hh * HQC:(hh + 1) * HQC],
                                     s4o[:, hh * HQC:(hh + 1) * HQC],
                                     waits=[e_dg], ch="dbg")
                guard["s4rb"] = k.dma("sync", pv_dbg[64:65, hh * HQC:(hh + 1) * HQC],
                                      s4rb[:, hh * HQC:(hh + 1) * HQC],
                                      waits=[e_dg2], ch="dbg")
        e_l = None
        for hh in range(2):
            e_l = k.op("scalar", lambda e, hh=hh: e.activation(
                s4r[:, hh * HQC:(hh + 1) * HQC], pPV[hh][64:65, :], AF.Ln),
                waits=[e_pv, guard.get("s4r")], incs=[("act", 1)])
        e_e = k.op("scalar", lambda e: e.activation(s4r[:], s4r[:], AF.Exp,
                                                    scale=-1.0),
                   waits=[e_l], incs=[("act", 1)])
        e_cb = k.op("vector", lambda e: e.tensor_copy(s4rb[:], s4r[:]),
                    waits=[e_e, guard.get("s4rb")], incs=[("dve", 1)])
        guard["s4r"] = e_cb
        e_rs = k.dma("sync", rcp_sp, s4rb[:], waits=[e_cb], ch="rcp")
        e_re = k.dma("sync", s4rep[:],
                     rcp_sp.unsqueeze(0).broadcast_to((64, 1, QC)),
                     waits=[e_rs, guard.get("s4rep")], ch="rcp")
        guard["s4rb"] = e_rs
        e_at = None
        for hh in range(2):
            e_at = k.op("vector", lambda e, hh=hh: e.tensor_tensor(
                s4o[:, hh * HQC:(hh + 1) * HQC], pPV[hh][0:64, :],
                s4rep[:, hh * HQC:(hh + 1) * HQC], ALU.mult),
                waits=[e_re, guard.get("s4o")], incs=[("dve", 1)])
        guard["pPV"] = e_at
        guard["s4rep"] = e_at
        h = kvh * 4 + hg
        ev_sp = k.dma("gpsimd", aT_sp[h * 64:(h + 1) * 64, qc * QC:(qc + 1) * QC],
                      s4o[:], waits=[e_at], ch="spa")
        guard["s4o"] = ev_sp
        guard[("qgb", ub)] = e_pv
        guard["kt_t"] = e_pv
        guard["vaug"] = e_pv
        return ev_sp

    ev_y = []
    for e2 in range(2):
        guard[("hsv", e2, 1)] = k.op(
            "vector", lambda e, e2=e2: e.memset(
                hsv[e2][1][:, (SCT - 1) * 512:SCT * 512], 0.0),
            incs=[("dve", 1)])

    scp = {}

    def scan_l(ksc):
        db = ksc % 2
        t0 = SCAN0 + ksc * SCT
        tsl = slice(t0, t0 + SCT)
        wd = [guard.get(("dstg", 0, db)), guard.get(("dstg", 1, db))]
        wx = [guard.get(("xstg", 0, db)), guard.get(("xstg", 1, db))]
        evd = evx = None
        for ds4 in range(4):
            dsrc = dTt_sp[tsl, ds4 * 256:(ds4 + 1) * 256] \
                .unsqueeze(0).broadcast_to((32, SCT, 256))
            evd = k.dma("sync", dstg[db][ds4 * 32:(ds4 + 1) * 32, :], dsrc,
                        waits=wd, ch="sdl%d" % db)
            xsrc = dxTt_sp[tsl, ds4 * 256:(ds4 + 1) * 256] \
                .unsqueeze(0).broadcast_to((32, SCT, 256))
            evx = k.dma("scalar", xstg[db][ds4 * 32:(ds4 + 1) * 32, :], xsrc,
                        waits=wx, ch="sxl%d" % db)
        bb = (ksc // BSB) % 2
        if ksc % BSB == 0:
            tbs = slice(t0, t0 + BSB * SCT)
            wb = [guard.get(("bstg", 0, bb)), guard.get(("bstg", 1, bb))]
            wc = [guard.get(("cstg", 0, bb)), guard.get(("cstg", 1, bb))]
            BST = BSB * SCT
            for nu4 in range(4):
                bsrc = BT_sp[nu4 * 32:(nu4 + 1) * 32, tbs] \
                    .unsqueeze(0).broadcast_to((4, 32, BST))
                guard[("bload", bb)] = k.dma(
                    "sync", bstg[bb][:, nu4 * BST:(nu4 + 1) * BST], bsrc,
                    waits=wb, ch="sbl%d" % bb)
                if ksc + BSB > KSC_W0:
                    csrc = CT_sp[nu4 * 32:(nu4 + 1) * 32, tbs] \
                        .unsqueeze(0).broadcast_to((4, 32, BST))
                    guard[("cload", bb)] = k.dma(
                        "scalar", cstg[bb][:, nu4 * BST:(nu4 + 1) * BST], csrc,
                        waits=wc, ch="scl%d" % bb)
        ee = {}
        for e2, (eng, ch) in enumerate(SENG):
            o0 = e2 * 128
            lv = lbuf[e2][db][:].rearrange("p (t nu o) -> p t nu o",
                                           t=SCT, nu=4)
            din = dstg[db][:].rearrange("p (t o) -> p t o", t=SCT) \
                [:, :, o0:o0 + 128].unsqueeze(2) \
                .broadcast_to((128, SCT, 4, 128))
            ain = c_Apairs[:].rearrange("p (nu o) -> p nu o", nu=4) \
                [:, :, o0:o0 + 128].unsqueeze(1) \
                .broadcast_to((128, SCT, 4, 128))
            e_l = k.op(eng, lambda e, lv=lv, din=din, ain=ain: e.tensor_tensor(
                lv, din, ain, ALU.mult),
                waits=[evd, ev_setup[1], guard.get(("lbuf", e2, db))],
                incs=[(ch, 1)])
            e_e = k.op("scalar", lambda e, e2=e2, db=db: e.activation(
                abuf[e2][db][:], lbuf[e2][db][:], AF.Exp),
                waits=[e_l, guard.get(("abuf", e2, db))], incs=[("act", 1)])
            guard[("dstg", e2, db)] = e_l
            ee[e2] = e_e
        scp[ksc] = (evx, guard.get(("bload", bb)), guard.get(("cload", bb)),
                    bb, ee)

    def scan_rest(ksc):
        evx, evb, evc2, bb, ee = scp.pop(ksc)
        db = ksc % 2
        hb = ksc % 2
        do_w = ksc >= KSC_W0
        tb0 = (ksc % BSB) * SCT
        for e2, (eng, ch) in enumerate(SENG):
            o0 = e2 * 128
            e_e = ee[e2]
            lv = lbuf[e2][db][:].rearrange("p (t nu o) -> p t nu o",
                                           t=SCT, nu=4)
            xin_ = xstg[db][:].rearrange("p (t o) -> p t o", t=SCT) \
                [:, :, o0:o0 + 128].unsqueeze(2) \
                .broadcast_to((128, SCT, 4, 128))
            bin_ = bstg[bb][:].rearrange("p (nu tb) -> p tb nu", nu=4) \
                [:, tb0:tb0 + SCT, :].unsqueeze(3) \
                .broadcast_to((128, SCT, 4, 128))
            e_u = k.op(eng, lambda e, lv=lv, xin_=xin_, bin_=bin_:
                       e.tensor_tensor(lv, xin_, bin_, ALU.mult),
                       waits=[evx, evb, e_e], incs=[(ch, 1)])
            prev = None
            for t in range(SCT):
                hp = hsv[e2][1 - hb][:, (SCT - 1) * 512:SCT * 512] if t == 0 \
                    else hsv[e2][hb][:, (t - 1) * 512:t * 512]
                w_hm = [e_e, e_u, guard.get(("hsv", e2, hb))] if t == 0 \
                    else [prev]
                e_hm = k.op(eng, lambda e, e2=e2, db=db, t=t, hp=hp:
                            e.tensor_tensor(hmt[e2][:], hp,
                                            abuf[e2][db][:, t * 512:(t + 1) * 512],
                                            ALU.mult),
                            waits=w_hm, incs=[(ch, 1)])
                prev = k.op(eng, lambda e, e2=e2, hb=hb, db=db, t=t:
                            e.tensor_tensor(hsv[e2][hb][:, t * 512:(t + 1) * 512],
                                            hmt[e2][:],
                                            lbuf[e2][db][:, t * 512:(t + 1) * 512],
                                            ALU.add),
                            waits=[e_hm], incs=[(ch, 1)])
            guard[("abuf", e2, db)] = prev
            guard[("hsv", e2, 1 - hb)] = prev
            if do_w:
                w4 = lv
                hv = hsv[e2][hb][:].rearrange("p (t nu o) -> p t nu o",
                                              t=SCT, nu=4)
                cin_ = cstg[bb][:].rearrange("p (nu tb) -> p tb nu", nu=4) \
                    [:, tb0:tb0 + SCT, :].unsqueeze(3) \
                    .broadcast_to((128, SCT, 4, 128))
                e_w4 = k.op(eng, lambda e, w4=w4, hv=hv, cin_=cin_:
                            e.tensor_tensor(w4, hv, cin_, ALU.mult),
                            waits=[prev, evc2], incs=[(ch, 1)])
                e_w2 = k.op(eng, lambda e, w4=w4: e.tensor_tensor(
                    w4[:, :, 0:2, :], w4[:, :, 0:2, :], w4[:, :, 2:4, :],
                    ALU.add), waits=[e_w4], incs=[(ch, 1)])
                toff = (ksc % CPB) * SCT
                wv = wringb[e2][:].rearrange("p (o T) -> p T o", T=TPE) \
                    [:, toff:toff + SCT, :]
                e_wr = k.op(eng, lambda e, wv=wv, w4=w4: e.tensor_tensor(
                    wv, w4[:, :, 0, :], w4[:, :, 1, :], ALU.add),
                    waits=[e_w2, guard.get(("wring", e2))], incs=[(ch, 1)])
                guard[("lbuf", e2, db)] = e_wr
                guard[("hsv", e2, hb)] = e_w4
                guard[("wrw", e2, ksc % CPB)] = e_wr
            else:
                guard[("lbuf", e2, db)] = prev
                guard[("hsv", e2, hb)] = prev
            guard[("xstg", e2, db)] = e_u
            guard[("bstg", e2, bb)] = e_u
            if do_w:
                guard[("cstg", e2, bb)] = e_w4

    def scan_yred(blk):
        ts = blk * TPE
        for e2, (eng, ch) in enumerate(SENG):
            wl = [guard.get(("wrw", e2, i)) for i in range(CPB)]
            for grp in range(4):
                e_mm = None
                for half in range(2):
                    for m in range(16):
                        ol = grp * 32 + half * 16 + m
                        e_mm = k.op("tensor", lambda e, e2=e2, half=half, m=m,
                                    ol=ol: e.matmul(
                                        pY[e2][64 * half:64 * (half + 1), :],
                                        C["r8st"][:, m * 64:(m + 1) * 64],
                                        wringb[e2][:, ol * TPE:(ol + 1) * TPE],
                                        start=(m == 0), stop=(m == 15)),
                                    waits=wl + ([guard.get(("pY", e2))]
                                                if half == 0 and m == 0 else []),
                                    incs=[("pe", 1)] if m == 15 else ())
                e_ev = k.op("vector", lambda e, e2=e2: e.tensor_copy(
                    ystg[e2][:], pY[e2][:]), waits=[e_mm, guard.get(("ystg", e2))],
                    incs=[("dve", 1)])
                guard[("pY", e2)] = e_ev
                base = e2 * 128 + grp * 32
                ydst = y_own.rearrange("(ds q) T -> ds q T", ds=4) \
                    [:, base:base + 32, ts:ts + TPE] \
                    .rearrange("ds (half m) T -> half m ds T", half=2)
                ev = k.dma("gpsimd", ydst, ystg[e2][:],
                           waits=[e_ev], ch="spy%d" % e2)
                guard[("ystg", e2)] = ev
                ev_y.append(ev)
            guard[("wring", e2)] = e_mm

    units = [(kvh, hg, qc) for kvh in range(4) for hg in range(4)
             for qc in range(QCN)]
    n_units = len(units)
    n_ticks = n_units * KTN
    n_steps = NSC + 1
    sc_state = [0, 0]

    def sc_step(j):
        if j < NSC:
            scan_l(j)
        if j >= 1:
            ksc = j - 1
            scan_rest(ksc)
            if ksc >= KSC_W0 and (ksc + 1 - KSC_W0) % CPB == 0:
                scan_yred((ksc + 1 - KSC_W0) // CPB - 1)

    def tick():
        sc_state[1] += 1
        while (sc_state[0] < n_steps
               and sc_state[0] * n_ticks <= sc_state[1] * n_steps):
            sc_step(sc_state[0])
            sc_state[0] += 1

    ev_attn = []
    for ui, (kvh, hg, qc) in enumerate(units):
        ev_attn.append(attn_unit(kvh, hg, qc, hg == 0 and qc == 0))
    while sc_state[0] < n_steps:
        sc_step(sc_state[0])
        sc_state[0] += 1

    s45.close()
    ev_y_fin = [("spy0", k.cnt["spy0"]), ("spy1", k.cnt["spy1"])]
    k.fence(ev_attn + ev_y_fin)
    if upto <= 5:
        return finish([])
    s78 = contextlib.ExitStack()
    s7s = contextlib.ExitStack()
    S7L = cfg.S8L
    FI = cfg.FI
    OC = FI // 128
    wst4 = [sb("wst4_%d" % i, (128, 128), BF16, scope=s78) for i in range(32)]
    xqs = [sb("xqs%d" % j, (128, S7L), BF16, scope=s7s) for j in range(DCH)]
    s7sg = sb("s7sg", (128, S7L), scope=s7s)
    s7h = sb("s7h", (128, S7L), scope=s7s)
    htok = [sb("htok%d" % q, (128, FI), scope=s7s) for q in range(S7L // 128)]
    hqtk = sb("hqtk", (128, FI), BF16, scope=s7s)
    hqT = [sb("hqT%d" % o, (128, S7L), BF16, scope=s7s) for o in range(OC)]
    s7o = sb("s7o", (128, S7L), BF16, scope=s7s)
    qb7 = QBufs(k, sb, s7s, FI, with_st=False)
    psA = ps("psA", (128, S7L), scope=s7s)
    psB = ps("psB", (128, S7L), scope=s7s)
    p_t7 = ps("p_t7", (128, 128), scope=s7s)
    p_t7b = ps("p_t7b", (128, 128), BF16, scope=s7s)

    used4 = []

    def stat_load(src_ap, waits=()):
        sti = wseq[0] % 32
        wseq[0] += 1
        used4.append(sti)
        q = "sync" if src_ap.dtype == BF16 else "gpsimd"
        ev = k.dma(q, wst4[sti][:], src_ap,
                   waits=[guard.get(("wst4", sti))] + list(waits),
                   ch="w4l%d%s" % (gseq[0] & 3, q[0]))
        return sti, ev

    def stat_load_t(dram, r0, c0, waits=()):
        sti = wseq[0] % 32
        wseq[0] += 1
        used4.append(sti)
        ev = k.dmat("sync", wst4[sti][:], dram[r0:r0 + 128, c0:c0 + 128],
                    waits=[guard.get(("wst4", sti))] + list(waits),
                    ch="w4l%ds" % (gseq[0] & 3))
        return sti, ev

    def stat_done(ev):
        gseq[0] += 1
        while used4:
            guard[("wst4", used4.pop())] = ev

    ev_gT = []
    for ts in range(TOWN // S7L):
        tsl = slice(ts * S7L, (ts + 1) * S7L)
        evx = [k.dma("scalar", xqs[j][:], xqT_sp[j * 128:(j + 1) * 128, tsl],
                     waits=[guard.get(("xqs", j))], ch="xqsl")
               for j in range(DCH)]
        for oc in range(OC):
            lds = [stat_load_t(wq4["wg4"], oc * 128, j * 128) for j in range(DCH)]
            evw = lds[-1][1]
            e_a = None
            for j, (sti, _) in enumerate(lds):
                e_a = k.op("tensor", lambda e, sti=sti, j=j: e.matmul(
                    psA[:], wst4[sti][:], xqs[j][:], start=(j == 0),
                    stop=(j == DCH - 1)),
                    waits=[evw] + evx + ([guard.get(id(psA))] if j == 0 else []),
                    incs=[("pe", 1)] if j == DCH - 1 else ())
            stat_done(e_a)
            lds = [stat_load_t(wq4["wu4"], oc * 128, j * 128) for j in range(DCH)]
            evw = lds[-1][1]
            e_b = None
            for j, (sti, _) in enumerate(lds):
                e_b = k.op("tensor", lambda e, sti=sti, j=j: e.matmul(
                    psB[:], wst4[sti][:], xqs[j][:], start=(j == 0),
                    stop=(j == DCH - 1)),
                    waits=[evw] + ([guard.get(id(psB))] if j == 0 else []),
                    incs=[("pe", 1)] if j == DCH - 1 else ())
            stat_done(e_b)
            e_si = k.op("scalar", lambda e: e.activation(s7sg[:], psA[:], AF.Silu),
                        waits=[e_a, guard.get("s7sg")], incs=[("act", 1)])
            guard[id(psA)] = e_si
            e_h = k.op("vector", lambda e: e.tensor_tensor(
                s7h[:], s7sg[:], psB[:], ALU.mult),
                waits=[e_si, e_b, guard.get("s7h")], incs=[("dve", 1)])
            guard[id(psB)] = e_h
            guard["s7sg"] = e_h
            e_hq = None
            for q in range(S7L // 128):
                e_t = k.op("tensor", lambda e, q=q: e.transpose(
                    p_t7[:], s7h[:, q * 128:(q + 1) * 128], c_idf[:]),
                    waits=[e_h, guard.get(id(p_t7))], incs=[("pe", 1)])
                e_ev = k.op("vector", lambda e, q=q, oc=oc: e.tensor_copy(
                    htok[q][:, oc * 128:(oc + 1) * 128], p_t7[:]),
                    waits=[e_t, guard.get(("htok", q))], incs=[("dve", 1)])
                guard[id(p_t7)] = e_ev
                guard[("htokw", q)] = e_ev
                e_hq = e_ev
            guard["s7h"] = e_hq
        hq_evs = {}
        for q in range(S7L // 128):
            evq = emit_quant(k, qb7, htok[q], hqtk, FI,
                             [guard.get("hqtk_sp"), guard.get(("htokw", q))])
            guard[("htok", q)] = evq
            for oc in range(OC):
                e_t = k.op("tensor", lambda e, oc=oc: e.transpose(
                    p_t7b[:], hqtk[:, oc * 128:(oc + 1) * 128], C["ident"][:]),
                    waits=[evq, guard.get(id(p_t7b)), guard.get(("hqT", oc))],
                    incs=[("pe", 1)])
                e_ev = k.op("vector", lambda e, oc=oc, q=q: e.tensor_copy(
                    hqT[oc][:, q * 128:(q + 1) * 128], p_t7b[:]),
                    waits=[e_t], incs=[("dve", 1)])
                guard[id(p_t7b)] = e_ev
                guard["hqtk_sp"] = e_ev
                hq_evs[oc] = e_ev
        for j8 in range(DCH):
            lds = [stat_load_t(wq4["wd4"], j8 * 128, oc * 128) for oc in range(OC)]
            evw = lds[-1][1]
            e_d = None
            for oc, (sti, _) in enumerate(lds):
                e_d = k.op("tensor", lambda e, sti=sti, oc=oc, j8=j8: e.matmul(
                    psA[:], wst4[sti][:], hqT[oc][:], start=(oc == 0),
                    stop=(oc == OC - 1)),
                    waits=[evw, hq_evs[oc]] +
                          ([guard.get(id(psA))] if oc == 0 else []),
                    incs=[("pe", 1)] if oc == OC - 1 else ())
            stat_done(e_d)
            e_g = k.op("scalar", lambda e: e.activation(s7o[:], psA[:], AF.Silu),
                       waits=[e_d, guard.get("s7o")], incs=[("act", 1)])
            guard[id(psA)] = e_g
            ev = k.dma("gpsimd", gT_sp[j8 * 128:(j8 + 1) * 128, tsl], s7o[:],
                       waits=[e_g], ch="spg")
            guard["s7o"] = ev
            ev_gT.append(ev)
        for oc in range(OC):
            guard[("hqT", oc)] = e_d
        for j in range(DCH):
            guard[("xqs", j)] = e_d

    s7s.close()
    k.fence(ev_gT)
    if upto <= 7:
        s78.close()
        return finish([])
    S8L = cfg.S8L
    yt = [sb("yt%d" % j, (128, TOWN), BF16, scope=s78) for j in range(DCH)]
    xt8 = [sb("xt8%d" % j, (128, S8L), scope=s78) for j in range(DCH)]
    at8 = [sb("at8%d" % j, (128, S8L), BF16, scope=s78) for j in range(DCH)]
    sim8 = [sb("sim8%d" % j, (128, S8L), BF16, scope=s78) for j in range(DCH)]
    x1t = [sb("x1t%d" % j, (128, S8L), scope=s78) for j in range(DCH)]
    xft = [sb("xft%d" % j, (128, S8L), BF16, scope=s78) for j in range(DCH)]
    ffb = [sb("ffb%d" % i, (128, S8L), BF16, scope=s78) for i in range(cfg.F // 128)]
    s8a = sb("s8a", (128, S8L), scope=s78)
    s8b = sb("s8b", (128, S8L), BF16, scope=s78)
    s8g = sb("s8g", (128, S8L), BF16, scope=s78)
    s8r = sb("s8r", (1, S8L), scope=s78)
    s8rb = sb("s8rb", (1, S8L), BF16, scope=s78)
    s8rep = sb("s8rep", (128, S8L), BF16, scope=s78)
    rst_sp = dint("rst_sp", (1, S8L), BF16)
    pX = ps("pX", (128, S8L), scope=s78)
    p_ss = ps("p_ss", (1, S8L), scope=s78)
    p_rp = ps("p_rp", (128, S8L), scope=s78)
    ev_out = []
    evy = [k.dma("gpsimd", yt[j][:], y_own[j * 128:(j + 1) * 128, :],
                 waits=ev_y_fin, ch="ytl") for j in range(DCH)]
    for ss in range(TOWN // S8L):
        tsl = slice(ss * S8L, (ss + 1) * S8L)
        evxt = [k.dma("sync", xt8[j][:], xT_own[j * 128:(j + 1) * 128, tsl],
                      waits=[guard.get(("xt8", j))], ch="xtl")
                for j in range(DCH)]
        evat = [k.dma("scalar", at8[j][:], aT_sp[j * 128:(j + 1) * 128, tsl],
                      waits=[guard.get(("at8", j))], ch="atl")
                for j in range(DCH)]
        ev_si = []
        for j in range(DCH):
            e = k.op("vector", lambda e, j=j, tsl=tsl: e.scalar_tensor_tensor(
                sim8[j][:], xt8[j][:], c_b[:, off_Dp + j:off_Dp + j + 1],
                yt[j][:, tsl], ALU.mult, ALU.add),
                waits=[evy[-1], evxt[-1], guard.get(("sim8", j))],
                incs=[("dve", 1)])
            ev_si.append(e)
        e_ss_done = None
        for j8 in range(DCH):
            lds = [stat_load(W["Wo"][j * 128:(j + 1) * 128,
                                     j8 * 128:(j8 + 1) * 128])
                   for j in range(DCH)]
            lds += [stat_load(W["Wout"][j * 128:(j + 1) * 128,
                                        j8 * 128:(j8 + 1) * 128])
                    for j in range(DCH)]
            evw = lds[-1][1]
            e_mm = None
            for j, (sti, _) in enumerate(lds[:DCH]):
                e_mm = k.op("tensor", lambda e, sti=sti, j=j: e.matmul(
                    pX[:], wst4[sti][:], at8[j][:], start=(j == 0), stop=False),
                    waits=[evw, evat[-1]] + ([guard.get(id(pX))] if j == 0 else []),
                    incs=())
            for j, (sti, _) in enumerate(lds[DCH:]):
                e_mm = k.op("tensor", lambda e, sti=sti, j=j: e.matmul(
                    pX[:], wst4[sti][:], sim8[j][:], start=False,
                    stop=(j == DCH - 1)),
                    waits=[evw, ev_si[j]],
                    incs=[("pe", 1)] if j == DCH - 1 else ())
            stat_done(e_mm)
            e_x1 = k.op("vector", lambda e, j8=j8: e.scalar_tensor_tensor(
                x1t[j8][:], pX[:], 0.5, xt8[j8][:], ALU.mult, ALU.add),
                waits=[e_mm, evxt[-1], guard.get(("x1t", j8))], incs=[("dve", 1)])
            guard[id(pX)] = e_x1
            guard[("sim8", j8)] = e_x1
            e_sq = k.op("scalar", lambda e, j8=j8: e.activation(
                s8b[:], x1t[j8][:], AF.Square),
                waits=[e_x1, guard.get("s8b")], incs=[("act", 1)])
            e_ss_mm = k.op("tensor", lambda e, j8=j8: e.matmul(
                p_ss[:], C["onescol"][:], s8b[:], start=(j8 == 0),
                stop=(j8 == DCH - 1)),
                waits=[e_sq] + ([guard.get(id(p_ss))] if j8 == 0 else []),
                incs=[("pe", 1)] if j8 == DCH - 1 else ())
            guard["s8b"] = e_ss_mm
            e_ss_done = e_ss_mm
        for j in range(DCH):
            guard[("sim8", j)] = e_ss_done
            guard[("at8", j)] = e_ss_done
        e_l = k.op("scalar", lambda e: e.activation(
            s8r[:], p_ss[:], AF.Ln, scale=1.0 / D, bias=cfg.EPS_RMS),
            waits=[e_ss_done, guard.get("s8r")], incs=[("act", 1)])
        e_e = k.op("scalar", lambda e: e.activation(
            s8r[:], s8r[:], AF.Exp, scale=-0.5), waits=[e_l], incs=[("act", 1)])
        guard[id(p_ss)] = e_e
        e_rb = k.op("vector", lambda e: e.tensor_copy(s8rb[:], s8r[:]),
                    waits=[e_e, guard.get("s8rb")], incs=[("dve", 1)])
        guard["s8r"] = e_rb
        e_rs8 = k.dma("sync", rst_sp, s8rb[:], waits=[e_rb], ch="rcp")
        e_rpe = k.dma("sync", s8rep[:],
                      rst_sp.unsqueeze(0).broadcast_to((128, 1, S8L)),
                      waits=[e_rs8, guard.get("s8rep")], ch="rcp")
        guard["s8rb"] = e_rs8
        ev_xf = []
        for j in range(DCH):
            e = k.op("vector", lambda e, j=j: e.tensor_tensor(
                xft[j][:], x1t[j][:], s8rep[:], ALU.mult),
                waits=[e_rpe, guard.get(("xft", j))], incs=[("dve", 1)])
            ev_xf.append(e)
        guard["s8rep"] = ev_xf[-1]
        FCH = cfg.F // 128
        for fc in range(FCH):
            lds = [stat_load(wfold["W1"][j * 128:(j + 1) * 128,
                                         fc * 128:(fc + 1) * 128])
                   for j in range(DCH)]
            evw = lds[-1][1]
            e_a = None
            for j, (sti, _) in enumerate(lds):
                e_a = k.op("tensor", lambda e, sti=sti, j=j: e.matmul(
                    pX[:], wst4[sti][:], xft[j][:], start=(j == 0),
                    stop=(j == DCH - 1)),
                    waits=[evw, ev_xf[j]] + ([guard.get(id(pX))] if j == 0 else []),
                    incs=[("pe", 1)] if j == DCH - 1 else ())
            stat_done(e_a)
            e_si = k.op("scalar", lambda e: e.activation(s8a[:], pX[:], AF.Silu),
                        waits=[e_a, guard.get("s8a")], incs=[("act", 1)])
            guard[id(pX)] = e_si
            lds = [stat_load(wfold["W1"][j * 128:(j + 1) * 128,
                                         cfg.F + fc * 128:cfg.F + (fc + 1) * 128])
                   for j in range(DCH)]
            evw = lds[-1][1]
            e_b = None
            for j, (sti, _) in enumerate(lds):
                e_b = k.op("tensor", lambda e, sti=sti, j=j: e.matmul(
                    pX[:], wst4[sti][:], xft[j][:], start=(j == 0),
                    stop=(j == DCH - 1)),
                    waits=[evw] + ([guard.get(id(pX))] if j == 0 else []),
                    incs=[("pe", 1)] if j == DCH - 1 else ())
            stat_done(e_b)
            e_ff = k.op("vector", lambda e, fc=fc: e.tensor_tensor(
                ffb[fc][:], s8a[:], pX[:], ALU.mult),
                waits=[e_b, e_si, guard.get(("ffb", fc))], incs=[("dve", 1)])
            guard[id(pX)] = e_ff
            guard["s8a"] = e_ff
            guard[("ffb+", fc)] = e_ff
        for j8 in range(DCH):
            e_mm = None
            for half in range(FCH // 16):
                lds = [stat_load(W["W2"][fc * 128:(fc + 1) * 128,
                                         j8 * 128:(j8 + 1) * 128])
                       for fc in range(half * 16, (half + 1) * 16)]
                evw = lds[-1][1]
                for i2, (sti, _) in enumerate(lds):
                    fc = half * 16 + i2
                    e_mm = k.op("tensor", lambda e, sti=sti, fc=fc: e.matmul(
                        pX[:], wst4[sti][:], ffb[fc][:], start=(fc == 0),
                        stop=(fc == FCH - 1)),
                        waits=[evw, guard.get(("ffb+", fc))] +
                              ([guard.get(id(pX))] if fc == 0 else []),
                        incs=[("pe", 1)] if fc == FCH - 1 else ())
                stat_done(e_mm)
            evg = k.dma("scalar", s8g[:], gT_sp[j8 * 128:(j8 + 1) * 128, tsl],
                        waits=ev_gT + [guard.get("s8g")], ch="s8gl")
            e_s = k.op("vector", lambda e, j8=j8: e.scalar_tensor_tensor(
                s8a[:], pX[:], 1.0, x1t[j8][:], ALU.mult, ALU.add),
                waits=[e_mm, guard.get("s8a")], incs=[("dve", 1)])
            guard[id(pX)] = e_s
            guard[("x1t", j8)] = e_s
            e_q = k.op("vector", lambda e: e.tensor_tensor(
                s8a[:], s8a[:], s8g[:], ALU.mult),
                waits=[evg, e_s], incs=[("dve", 1)])
            guard["s8g"] = e_q
            e_o = k.op("vector", lambda e, j8=j8: e.tensor_tensor(
                s8a[:], s8a[:], xt8[j8][:], ALU.add),
                waits=[e_q], incs=[("dve", 1)])
            guard[("xt8", j8)] = e_o
            ev = k.dma("gpsimd", outT[j8 * 128:(j8 + 1) * 128, tsl], s8a[:],
                       waits=[e_o], ch="spo")
            guard["s8a"] = ev
            guard[("xft", j8)] = e_mm
            ev_out.append(ev)
        for fc in range(FCH):
            guard[("ffb", fc)] = e_mm
    k.op("gpsimd", lambda e: None, waits=ev_out)

    s78.close()
    blk = es.enter_context(nc.Block())
    for eng in ENGINES:
        def runner(e, _eng=eng):
            for f in k.ops[_eng]:
                f(e)
        getattr(blk, eng)(runner)
    es.close()
    return nc



class QBufs:

    def __init__(self, k, sb, scope, width, with_st=True):
        n = QBufs._n = getattr(QBufs, "_n", 0) + 1
        if with_st:
            self.st = sb("qb_st%d" % n, (128, width), F32, scope=scope)
            self.ob = sb("qb_ob%d" % n, (128, width), BF16, scope=scope)
        self.tmp = sb("qb_tmp%d" % n, (128, width), F32, scope=scope)
        self.r = sb("qb_r%d" % n, (128, width), F32, scope=scope)
        self.bm = sb("qb_bm%d" % n, (128, 2 * (width // 16)), F32, scope=scope)
        self.q2 = sb("qb_q2%d" % n, (128, width), F32, scope=scope)
        self.sgn = sb("qb_sgn%d" % n, (128, width), I32, scope=scope)
        self.mi = sb("qb_mi%d" % n, (128, width), I32, scope=scope)
        self.m = self.tmp
        self.guard = None


def emit_quant(k, qb, src, out_bf, width, waits):
    nb = width // 16
    CC = float((1 << 22) + 1)
    w = [x for x in waits] + [qb.guard]
    blk3 = lambda ap: ap.rearrange("p (b k) -> p b k", k=16)
    bc3 = lambda ap: ap.unsqueeze(-1).broadcast_to((128, nb, 16))
    e1 = k.op("vector", lambda e: e.tensor_reduce(
        qb.bm[:, 0:nb], blk3(src[:, 0:width]), AX.X, ALU.max,
        apply_absolute_value=True), waits=w, incs=[("dve", 1)])
    e2 = k.op("vector", lambda e: e.tensor_scalar_max(
        qb.bm[:, 0:nb], qb.bm[:, 0:nb], 6e-12), waits=[e1], incs=[("dve", 1)])
    e3 = k.op("scalar", lambda e: e.activation(
        qb.bm[:, nb:2 * nb], qb.bm[:, 0:nb], AF.Ln), waits=[e2], incs=[("act", 1)])
    e4 = k.op("scalar", lambda e: e.activation(
        qb.bm[:, nb:2 * nb], qb.bm[:, nb:2 * nb], AF.Exp, scale=-1.0),
        waits=[e3], incs=[("act", 1)])
    e5 = k.op("vector", lambda e: e.tensor_scalar_mul(
        qb.bm[:, nb:2 * nb], qb.bm[:, nb:2 * nb], 6.0), waits=[e4],
        incs=[("dve", 1)])
    e6 = k.op("vector", lambda e: e.tensor_tensor(
        blk3(qb.tmp[:, 0:width]), blk3(src[:, 0:width]), bc3(qb.bm[:, nb:2 * nb]),
        ALU.mult), waits=[e5], incs=[("dve", 1)])
    mask = k.sgnmask_col()
    e7 = k.op("vector", lambda e: e.tensor_scalar(
        qb.sgn[:, 0:width], qb.tmp[:, 0:width].bitcast(I32), mask,
        None, ALU.bitwise_and), waits=[e6, k._sgn_ev], incs=[("dve", 1)])
    pmask = k.posmask_col()
    e8 = k.op("vector", lambda e: e.tensor_scalar(
        qb.tmp[:, 0:width].bitcast(I32), qb.tmp[:, 0:width].bitcast(I32),
        pmask, None, ALU.bitwise_and), waits=[e7], incs=[("dve", 1)])
    e9 = k.op("vector", lambda e: e.scalar_tensor_tensor(
        qb.r[:, 0:width], qb.tmp[:, 0:width], CC, qb.tmp[:, 0:width],
        ALU.mult, ALU.subtract), waits=[e8], incs=[("dve", 1)])
    e10 = k.op("vector", lambda e: e.scalar_tensor_tensor(
        qb.r[:, 0:width], qb.tmp[:, 0:width], CC, qb.r[:, 0:width],
        ALU.mult, ALU.subtract), waits=[e9], incs=[("dve", 1)])
    C23 = float(1 << 23)
    e11 = k.op("vector", lambda e: e.tensor_scalar(
        qb.q2[:, 0:width], qb.tmp[:, 0:width], 2.0, C23, ALU.mult, ALU.add),
        waits=[e10], incs=[("dve", 1)])
    e12 = k.op("vector", lambda e: e.tensor_scalar(
        qb.q2[:, 0:width], qb.q2[:, 0:width], C23, 0.5, ALU.subtract, ALU.mult),
        waits=[e11], incs=[("dve", 1)])
    e12b = k.op("vector", lambda e: e.tensor_scalar(
        qb.mi[:, 0:width], qb.tmp[:, 0:width], 2.0, None, ALU.is_lt),
        waits=[e12], incs=[("dve", 1)])
    e13 = k.op("vector", lambda e: e.copy_predicated(
        qb.r[:, 0:width], qb.mi[:, 0:width], qb.q2[:, 0:width]),
        waits=[e12b], incs=[("dve", 1)])
    e14 = k.op("vector", lambda e: e.tensor_scalar_mul(
        qb.bm[:, 0:nb], qb.bm[:, 0:nb], 1.0 / 6.0), waits=[e13], incs=[("dve", 1)])
    e15 = k.op("vector", lambda e: e.tensor_tensor(
        blk3(qb.r[:, 0:width]), blk3(qb.r[:, 0:width]), bc3(qb.bm[:, 0:nb]),
        ALU.mult), waits=[e14], incs=[("dve", 1)])
    e16 = k.op("vector", lambda e: e.tensor_tensor(
        qb.r[:, 0:width].bitcast(I32), qb.r[:, 0:width].bitcast(I32),
        qb.sgn[:, 0:width], ALU.bitwise_or), waits=[e15], incs=[("dve", 1)])
    e17 = k.op("vector", lambda e: e.tensor_copy(
        out_bf[:, 0:width], qb.r[:, 0:width]), waits=[e16], incs=[("dve", 1)])
    qb.guard = e17
    return e17
